# version 03
# baseline (speedup 1.0000x reference)
import numpy as np
import ml_dtypes
import concourse.bass as bass
import concourse.mybir as mybir
from concourse.bass_utils import run_bass_kernel_spmd

F32 = mybir.dt.float32
BF16 = mybir.dt.bfloat16
AF = mybir.ActivationFunctionType
ALU = mybir.AluOpType
AX = mybir.AxisListType

RC = 12582912.0
DM = 2048
NH = 16
NT = 8
HPG = 4
NG = NH // HPG
EPS_LN = 1e-5
EPS_DEN = 1e-6
QB = 127.0
HID = 8192
WGV_ELEMS = 16384 * 2048
WOF_ELEMS = 2048 * 8192

_off = 16512
def _nx(sz):
    global _off
    o = _off
    _off = (_off + sz + 31) & ~31
    return o

O_EYE = _nx(256)
O_WCOL = _nx(256)
O_D128 = _nx(32)
O_LND = _nx(64)
O_SCAL = _nx(128)
O_STATS = _nx(224)
O_ABSACC = _nx(768)
O_CRC = _nx(32)
O_ONES = _nx(32)
O_FL = _nx(32)
O_FL2 = _nx(32)
O_FLS = _nx(32)
O_WK = _nx(8192)
O_JK = _nx(8192)
O_STAGE = _nx(16384)
O_RING = _nx(16384)
PH = (_off + 31) & ~31
O_MM = O_STAGE
O_MP = O_STAGE + 512
O_MC = O_STAGE + 1024
O_UT = O_STAGE + 1088
O_TRI = O_STAGE + 1664
O_UT2 = O_STAGE + 1920
O_U = O_WK
O_DPOW = O_WK + 4096
O_SSB = PH
O_LNXT = O_SSB + 8192
O_WRING = O_LNXT + 65536
O_QKB = O_WRING + 49152
O_AO = O_QKB + 6144
assert O_AO + 32768 <= 229376, (PH, O_AO + 32768)
O_ONESR = O_AO + 1024
O_WOA = PH
O_WOB = O_WOA + 49152
assert O_WOB + 16384 <= 229376
O_H = PH
O_A1T = O_RING
O_WT = O_H + 131072
O_GREP = O_WT
O_SPARE = O_WT + 16384
O_SILU = O_SPARE + 12288
assert O_SILU + 1024 <= 229376
assert O_SPARE + 12288 <= 229376, O_SPARE + 12288


class Prog:
    ENGINES = ("sync", "gpsimd", "scalar", "vector", "tensor")

    def __init__(self):
        self.items = []
        self.cnt = {}

    def _bump(self, sem, amt):
        self.cnt[sem] = self.cnt.get(sem, 0) + amt
        return (sem, self.cnt[sem])

    def dma(self, eng, out, in_, waits=()):
        tok = self._bump(eng + "_dma", 16)
        self.items.append((eng, tuple(w for w in waits if w),
                           (lambda e, o=out, i=in_: e.dma_start(out=o, in_=i)), tok))
        return tok

    def op(self, eng, fn, waits=(), token=False):
        tok = self._bump(eng + "_c", 1) if token else None
        self.items.append((eng, tuple(w for w in waits if w), fn, tok))
        return tok

    def _flfn(self, e):
        return e.tensor_scalar(self.FL2[:], self.FL[:, 0:1].broadcast_to((128, 8)),
                               1.0, None, mybir.AluOpType.mult)

    def vflush(self):
        self.items.append(("vector", (), self._flfn, None))

    def opf(self, eng, fn, waits=(), token=False):
        assert eng == "vector"
        self.items.append((eng, tuple(w for w in waits if w), fn, None))
        self.items.append((eng, (), self._flfn, None))
        tok = self._bump("vector_c", 1) if token else None
        self.items.append((eng, (), self._flfn, tok))
        return tok

    def vdelay(self, waits):
        self.items.append(("vector", tuple(w for w in waits if w), self._flfn, None))

    def tok_after(self, waits):
        tok = self._bump("vector_c", 1)
        self.items.append(("vector", tuple(w for w in waits if w), self._flfn, tok))
        return tok

    def sopf(self, eng, fn, waits=(), token=False):
        assert eng == "scalar"
        self.items.append((eng, tuple(w for w in waits if w), fn, None))
        self.items.append((eng, (),
                           (lambda e: e.copy(self.FLS[:], self.FL[:, 0:4])), None))
        tok = self._bump("scalar_c", 1) if token else None
        self.items.append((eng, (),
                           (lambda e: e.copy(self.FLS[:], self.FL[:, 0:4])), tok))
        return tok

    def sdelay(self, waits):
        self.items.append(("scalar", tuple(w for w in waits if w),
                           (lambda e: e.copy(self.FLS[:], self.FL[:, 0:4])), None))

    def build(self, block, sems):
        for engname in self.ENGINES:
            def mk(en):
                def body(eng):
                    hw = {}
                    own_c = en + "_c"
                    for e, waits, fn, inc in self.items:
                        if e != en:
                            continue
                        for s, v in waits:
                            if s == own_c or hw.get(s, -1) >= v:
                                continue
                            eng.wait_ge(sems[s], v)
                            hw[s] = v
                        ins = fn(eng)
                        if inc is not None:
                            ins.then_inc(sems[inc[0]],
                                         16 if inc[0].endswith("_dma") else 1)
                return body
            getattr(block, engname)(mk(engname))


def build_nc(debug=False, abs_disable=False):
    nc = bass.Bass("TRN2", target_bir_lowering=False, debug=False)

    xw = nc.dram_tensor("xw", [NT, 128, DM], F32, kind="ExternalInput").ap()
    xp = nc.dram_tensor("xp", [NT, 128, DM], F32, kind="ExternalInput").ap()
    wqkv = nc.dram_tensor("wqkv", [16, 128, 3 * DM], F32, kind="ExternalInput").ap()
    wouta = nc.dram_tensor("wouta", [16, 128, DM], F32, kind="ExternalInput").ap()
    wgv = nc.dram_tensor("wgv", [128, 128, DM], F32, kind="ExternalInput").ap()
    woutf = nc.dram_tensor("woutf", [16, 128, HID], F32, kind="ExternalInput").ap()
    dpin = nc.dram_tensor("dp", [1, NH], F32, kind="ExternalInput").ap()
    gwin = nc.dram_tensor("gw", [1, DM], F32, kind="ExternalInput").ap()
    pmin = nc.dram_tensor("pmask", [1, 1], F32, kind="ExternalInput").ap()
    y = nc.dram_tensor("y", [NT, 128, DM], F32, kind="ExternalOutput").ap()
    x2d = nc.dram_tensor("x2d", [NT, 128, DM], F32,
                         kind="ExternalOutput" if debug else "Internal").ap()
    wgvT_d = nc.dram_tensor("wgvT_d", [32, 128, 8192], BF16, kind="Internal").ap()
    x2ud = nc.dram_tensor("x2ud", [NT, 128, DM], F32, kind="Internal").ap()
    woT_d = nc.dram_tensor("woT_d", [16, 128, 8192], BF16, kind="Internal").ap()
    ao_d = None
    if debug:
        ao_d = nc.dram_tensor("ao_d", [NT, 128, DM], BF16, kind="ExternalOutput").ap()
        ao_g = nc.dram_tensor("ao_g", [4, 128, NT * DM], BF16, kind="ExternalOutput").ap()
        lnxt_d = nc.dram_tensor("lnxt_d", [128, 32768], BF16, kind="ExternalOutput").ap()
        u_d = nc.dram_tensor("u_d", [128, 2048], BF16, kind="ExternalOutput").ap()
        dpow_d = nc.dram_tensor("dpow_d", [1, 2048], BF16, kind="ExternalOutput").ap()
        wcol_d = nc.dram_tensor("wcol_d", [128, 128], BF16, kind="ExternalOutput").ap()
        lnd_d = nc.dram_tensor("lnd_d", [128, 16], F32, kind="ExternalOutput").ap()
        ssb_d = nc.dram_tensor("ssb_d", [1, 4096], BF16, kind="ExternalOutput").ap()
        ssbp_d = nc.dram_tensor("ssbp_d", [4, 1, 4096], BF16, kind="ExternalOutput").ap()
        scal_d = nc.dram_tensor("scal_d", [128, 16], F32, kind="ExternalOutput").ap()
        stats_d = nc.dram_tensor("stats_d", [128, 56], F32, kind="ExternalOutput").ap()
        h_d = nc.dram_tensor("h_d", [128, 8192], F32, kind="ExternalOutput").ap()
        a1t_d = nc.dram_tensor("a1t_d", [128, 8192], BF16, kind="ExternalOutput").ap()
        wt_d = nc.dram_tensor("wt_d", [128, 8192], BF16, kind="ExternalOutput").ap()
        a2t_d = nc.dram_tensor("a2t_d", [128, 8192], BF16, kind="ExternalOutput").ap()
        wotq_d = nc.dram_tensor("wotq_d", [128, 8192], BF16, kind="ExternalOutput").ap()
        h3_d = nc.dram_tensor("h3_d", [128, 8192], F32, kind="ExternalOutput").ap()
        h3b_d = nc.dram_tensor("h3b_d", [128, 8192], F32, kind="ExternalOutput").ap()
        statsb_d = nc.dram_tensor("statsb_d", [128, 56], F32, kind="ExternalOutput").ap()
        a2tb_d = nc.dram_tensor("a2tb_d", [128, 8192], BF16, kind="ExternalOutput").ap()

    ss_ = np.arange(128, dtype=np.float32)[:, None]
    tt_ = np.arange(128, dtype=np.float32)[None, :]
    Mmask = np.where(ss_ <= tt_, tt_ - ss_, 0.0).astype(np.float32)
    TRI01 = (ss_ <= tt_).astype(ml_dtypes.bfloat16)
    MP = (np.arange(128, dtype=np.float32) + 1.0).reshape(1, 128)
    MC = (1023.0 - 128.0 * np.arange(8, dtype=np.float32)[None, :]
          - np.arange(128, dtype=np.float32)[:, None]).astype(np.float32)
    d_M = nc.inline_tensor(Mmask, "c_M").ap()
    d_TRI = nc.inline_tensor(TRI01, "c_TRI").ap()
    d_MP = nc.inline_tensor(MP, "c_MP").ap()
    d_MC = nc.inline_tensor(MC, "c_MC").ap()
    d_EYE = nc.inline_tensor(np.eye(128, dtype=ml_dtypes.bfloat16), "c_EYE").ap()
    d_ONES = nc.inline_tensor(np.ones((128, 1), dtype=np.float32), "c_ONES").ap()

    def sbt(name, shape, dtype, off):
        return nc.alloc_sbuf_tensor_at(name, shape, dtype, offset=off).ap()

    EYE = sbt("EYE", [128, 128], BF16, O_EYE)
    MMc = sbt("MMc", [128, 128], F32, O_MM)
    MPr = sbt("MPr", [1, 128], F32, O_MP)
    MCc = sbt("MCc", [128, 8], F32, O_MC)
    ONES = sbt("ONESs", [128, 1], F32, O_ONES)
    WCOL = sbt("WCOL", [128, 128], BF16, O_WCOL)
    D128 = sbt("D128", [1, 16], BF16, O_D128)
    LND = sbt("LND", [128, NH], F32, O_LND)
    SCAL = sbt("SCAL", [128, 16], F32, O_SCAL)
    STATS = sbt("STATS", [128, 56], F32, O_STATS)
    ABSACC = sbt("ABSACC", [128, 192], F32, O_ABSACC)
    UT = sbt("UTs", [128, 128], F32, O_UT)
    TRI = sbt("TRI", [128, 128], BF16, O_TRI)
    UT2 = sbt("UT2", [128, 128], F32, O_UT2)
    SILU = sbt("SILU", [128, 256], F32, O_SILU)
    ONESRP = sbt("ONESRP", [1, 128], F32, O_ABSACC)
    CRC = sbt("CRC", [128, 1], F32, O_CRC)
    FL = sbt("FL", [128, 4], F32, O_FL)
    FL2 = sbt("FL2", [128, 8], F32, O_FL2)
    FLS = sbt("FLS", [128, 4], F32, O_FLS)
    STREAM = sbt("STREAM", [128, 4096], F32, O_STAGE)
    SPX = sbt("SPX", [128, 2048], F32, O_SPARE)
    SPO = sbt("SPO", [128, 1024], F32, O_SPARE + 8192)
    nc.const_aps.register(F32, RC, CRC[:])
    WK = sbt("WK", [128, 2048], F32, O_WK)
    JK = sbt("JK", [128, 4096], BF16, O_JK)
    JKF = sbt("JKF", [128, 2048], F32, O_JK)
    STAGE = sbt("STAGE", [128, 4096], F32, O_STAGE)
    RING = sbt("RING", [128, 4096], F32, O_RING)
    U = sbt("Utab", [128, 2048], BF16, O_U)
    DPOW = sbt("DPOW", [1, 2048], BF16, O_DPOW)
    SSB = sbt("SSB", [1, 4096], BF16, O_SSB)
    LNXT = sbt("LNXT", [128, 32768], BF16, O_LNXT)
    WRING = sbt("WRING", [128, 24576], BF16, O_WRING)
    QKB = sbt("QKB", [128, 3072], BF16, O_QKB)
    AO = sbt("AO", [128, 16384], BF16, O_AO)
    WOA = sbt("WOA", [128, 24576], BF16, O_WOA)
    WOB = sbt("WOB", [128, 8192], BF16, O_WOB)
    H = sbt("H", [128, 32768], F32, O_H)
    A1T = sbt("A1T", [128, 8192], BF16, O_A1T)
    WT = sbt("WT", [128, 8192], BF16, O_WT)
    GREP = sbt("GREP", [128, 2048], F32, O_GREP)
    WOTQ = [sbt(f"WOTQ{j}", [128, 8192], BF16, O_H + j * 32768) for j in range(4)]
    A2 = [sbt(f"A2_{t}", [128, 8192], BF16, O_H + t * 32768) for t in range(4)]
    A2T = [sbt(f"A2T{t}", [128, 8192], BF16, O_H + t * 32768 + 16384) for t in range(4)]

    p = Prog()
    p.FL = FL
    p.FL2 = FL2
    p.FLS = FLS

    with (
        nc.psum_tensor("P0", [128, 512], F32) as P0,
        nc.psum_tensor("P1", [128, 512], F32) as P1,
        nc.psum_tensor("P2", [128, 512], F32) as P2,
        nc.psum_tensor("P3", [128, 512], F32) as P3,
        nc.psum_tensor("P4", [128, 512], F32) as P4,
        nc.psum_tensor("P5", [128, 512], F32) as P5,
        nc.psum_tensor("P6", [128, 512], F32) as P6,
        nc.psum_tensor("PTb", [128, 1024], BF16) as PT,
        nc.semaphore("se_sy") as h_sy,
        nc.semaphore("se_gp") as h_gp,
        nc.semaphore("se_sq") as h_sq,
        nc.semaphore("se_t") as h_t,
        nc.semaphore("se_v") as h_v,
        nc.semaphore("se_s") as h_s,
        nc.Block() as block,
    ):
        sems = {"sync_dma": h_sy, "gpsimd_dma": h_gp, "scalar_dma": h_sq,
                "tensor_c": h_t, "vector_c": h_v, "scalar_c": h_s}
        PB = [P0, P1, P2, P3]

        tr_state = {"tok": None}

        def transpose16(src_ap_fn, dst_fn, src_tok, extra_wait=None, nblk=16,
                        evac_wait=None):
            evs = []
            for half in range(nblk // 8):
                w0 = [src_tok, tr_state["tok"]]
                if extra_wait:
                    w0 += extra_wait if isinstance(extra_wait, list) else [extra_wait]
                t_tr = None
                for j in range(8):
                    jj = half * 8 + j
                    t_tr = p.op("tensor",
                                lambda e, jj=jj, j=j: e.transpose(
                                    PT[:, j * 128:(j + 1) * 128], src_ap_fn(jj), EYE[:]),
                                waits=w0 if j == 0 else [], token=(j == 7))
                out_ap, in_ap = dst_fn(half)
                ew = [t_tr]
                if half == 0 and evac_wait:
                    ew += evac_wait if isinstance(evac_wait, list) else [evac_wait]
                ev = p.opf("vector", lambda e, o=out_ap, i=in_ap: e.tensor_copy(o, i),
                           waits=ew, token=True)
                tr_state["tok"] = ev
                evs.append(ev)
            return evs

        t_eye = p.dma("sync", EYE[:], d_EYE)
        t_mm = p.dma("sync", MMc[:], d_M)
        t_tri = p.dma("sync", TRI[:], d_TRI)
        t_mp = p.dma("sync", MPr[:], d_MP)
        t_mc = p.dma("sync", MCc[:], d_MC)
        t_ones = p.dma("sync", ONES[:], d_ONES)
        t_dp0 = p.dma("sync", LND[0:1, :], dpin)
        t_or = p.opf("vector", lambda e: e.memset(ONESRP[:], 1.0), token=True)
        t_dpg = p.tok_after([t_dp0, t_or])
        t_dpm = p.op("tensor", lambda e: e.matmul(
            P4[:, 0:NH], ONESRP[:], LND[0:1, :], start=True, stop=True),
            waits=[t_dpg], token=True)
        t_dp = p.opf("vector", lambda e: e.tensor_copy(LND[:], P4[:, 0:NH]),
                     waits=[t_dpm], token=True)
        t_pm = p.dma("sync", SCAL[0:1, 0:1], pmin)

        p.op("vector", lambda e: e.memset(CRC[:], RC), token=True)
        p.op("vector", lambda e: e.memset(FL[:], 1.0), token=True)
        p.sdelay([t_dp])
        t_sg = p.op("scalar", lambda e: e.activation(LND[:], LND[:], AF.Sigmoid),
                    waits=[t_dp], token=True)
        t_ln = p.op("scalar", lambda e: e.activation(LND[:], LND[:], AF.Ln),
                    waits=[t_sg], token=True)
        tab = t_ln
        p.vdelay([t_ln])
        for h in range(NH):
            t1 = p.opf("vector", lambda e, h=h: e.tensor_scalar(
                UT[:], MMc[:], LND[:, h:h + 1], -87.0, ALU.mult, ALU.max),
                waits=[t_mm, t_ln, tab], token=True)
            p.sdelay([t1])
            t1e = p.op("scalar", lambda e, h=h: e.activation(
                UT2[:], UT[:], AF.Exp), waits=[t1], token=True)
            p.vdelay([t1e])
            p.vflush()
            p.vflush()
            tab = p.opf("vector", lambda e, h=h: e.tensor_tensor(
                U[:, h * 128:(h + 1) * 128], UT2[:], TRI[:], ALU.mult),
                waits=[t1e, t_tri], token=True)
            t2 = p.opf("vector", lambda e, h=h: e.tensor_scalar(
                UT[0:1, :], MPr[:], LND[0:1, h:h + 1], -87.0, ALU.mult, ALU.max),
                waits=[t_mp, tab], token=True)
            p.sdelay([t2])
            tab = p.op("scalar", lambda e, h=h: e.activation(
                DPOW[0:1, h * 128:(h + 1) * 128], UT[0:1, :], AF.Exp),
                waits=[t2], token=True)
            t3 = p.opf("vector", lambda e, h=h: e.tensor_scalar(
                UT[:, 0:8], MCc[:], LND[:, h:h + 1], -87.0, ALU.mult, ALU.max),
                waits=[t_mc, tab], token=True)
            p.sdelay([t3])
            tab = p.op("scalar", lambda e, h=h: e.activation(
                WCOL[:, h * 8:(h + 1) * 8], UT[:, 0:8], AF.Exp),
                waits=[t3], token=True)
        t4 = p.opf("vector", lambda e: e.tensor_scalar(
            UT[0:1, 0:16], LND[0:1, :], 128.0, -87.0, ALU.mult, ALU.max),
            waits=[tab], token=True)
        tab = p.op("scalar", lambda e: e.activation(
            D128[0:1, :], UT[0:1, 0:16], AF.Exp), waits=[t4], token=True)

        abs_chunks = []
        for i in range(128):
            abs_chunks.append((wgv[i], i))
        for i in range(16):
            for j in range(4):
                abs_chunks.append((woutf[i][:, j * DM:(j + 1) * DM], 128 + 4 * i + j))
        abs_st = {"i": 0, "tok": [None, None]}

        def emit_abs(n, queue=None):
            if abs_disable:
                return
            for _ in range(n):
                i = abs_st["i"]
                if i >= len(abs_chunks):
                    return
                abs_st["i"] += 1
                src, acc = abs_chunks[i]
                sl = i % 2
                off = sl * 2048
                qn = queue if queue is not None else "sync"
                t_d = p.dma(qn, RING[:, off:off + DM], src,
                            waits=[abs_st["tok"][sl]])
                abs_st["tok"][sl] = p.op(
                    "scalar", lambda e, off=off, acc=acc: e.activation(
                        RING[:, off:off + DM], RING[:, off:off + DM], AF.Abs,
                        accum_out=ABSACC[:, acc:acc + 1]),
                    waits=[t_d], token=True)

        lnx_out = [None] * 16
        stg_tok = [None, None]
        sq_tok = [None, None]
        for i in range(16):
            slot = (8 + i) if i < 8 else (i - 8)
            src = xp[i] if i < 8 else xw[i - 8]
            buf = i % 2
            xoff = buf * 2048
            t_x = p.dma("sync", STAGE[:, xoff:xoff + DM], src,
                        waits=[stg_tok[buf], sq_tok[1 - buf]] + ([tab] if i < 2 else []))
            t_sum = p.op("vector", lambda e, xoff=xoff: e.reduce_sum(
                STATS[:, 0:1], STAGE[:, xoff:xoff + DM], AX.X),
                waits=[t_x], token=True)
            t_sq = p.op("scalar", lambda e, xoff=xoff, buf=buf: e.activation(
                STAGE[:, (1 - buf) * 2048:(1 - buf) * 2048 + DM],
                STAGE[:, xoff:xoff + DM], AF.Square, accum_out=STATS[:, 1:2]),
                waits=[t_x, stg_tok[1 - buf]], token=True)
            sq_tok[buf] = t_sq
            p.vdelay([t_sq])
            t_b = p.op("vector", lambda e: e.tensor_scalar(
                STATS[:, 5:6], STATS[:, 1:2], 1.0 / DM, EPS_LN, ALU.mult, ALU.add),
                waits=[t_sq], token=True)
            t_a = p.op("vector", lambda e: e.tensor_scalar(
                STATS[:, 4:5], STATS[:, 0:1], -1.0 / DM, None, ALU.mult),
                waits=[t_sum], token=True)
            p.vflush()
            t_c = p.op("vector", lambda e: e.tensor_tensor(
                STATS[:, 6:7], STATS[:, 4:5], STATS[:, 4:5], ALU.mult),
                waits=[t_a], token=True)
            p.vflush()
            t_d2 = p.opf("vector", lambda e: e.tensor_tensor(
                STATS[:, 7:8], STATS[:, 5:6], STATS[:, 6:7], ALU.subtract),
                waits=[t_b, t_c], token=True)
            p.sdelay([t_d2])
            t_e = p.op("scalar", lambda e: e.activation(
                STATS[:, 8:9], STATS[:, 7:8], AF.Ln), waits=[t_d2], token=True)
            t_f = p.op("scalar", lambda e: e.activation(
                STATS[:, 9:10], STATS[:, 8:9], AF.Exp, scale=-0.5),
                waits=[t_e], token=True)
            p.vdelay([t_f])
            t_n = p.opf("vector", lambda e, xoff=xoff, buf=buf: e.tensor_scalar(
                JK[:, buf * 2048:buf * 2048 + DM], STAGE[:, xoff:xoff + DM],
                STATS[:, 4:5], STATS[:, 9:10], ALU.add, ALU.mult),
                waits=[t_f, t_sq], token=True)
            stg_tok[buf] = t_n
            evs = transpose16(
                (lambda jj, buf=buf: JK[:, buf * 2048 + jj * 128:buf * 2048 + (jj + 1) * 128]),
                (lambda half, slot=slot: (
                    LNXT[:, slot * 2048 + half * 1024:slot * 2048 + (half + 1) * 1024],
                    PT[:])),
                t_n, extra_wait=t_eye)
            lnx_out[slot] = evs[-1]
            emit_abs(1)

        fill_j = [0]

        def pe_fill(n):
            for _ in range(n):
                j = fill_j[0] % 8
                fill_j[0] += 1
                p.op("tensor", lambda e, j=j: e.transpose(
                    PT[:, j * 128:(j + 1) * 128], LNXT[:, 0:128], EYE[:]),
                    waits=[], token=False)

        NSLOT = 16
        ring_dma = {}
        cons_tok = {}

        GORDER0 = [1, 2, 3, 0]

        def slot_of(gi, k):
            return (gi * 16 + k) % NSLOT

        def emit_wload(gi, k, extra=None):
            sl = slot_of(gi, k)
            g = GORDER0[gi]
            prev = (gi * 16 + k) - NSLOT
            waits = [extra]
            if prev >= 0:
                waits.append(cons_tok.get((prev // 16, prev % 16)))
            tok = None
            for part in range(3):
                src = wqkv[k][:, part * DM + g * 512: part * DM + (g + 1) * 512]
                tok = p.dma("gpsimd",
                            WRING[:, sl * 1536 + part * 512:sl * 1536 + (part + 1) * 512],
                            src, waits=waits if part == 0 else [])
            ring_dma[(gi, k)] = tok

        for k in range(16):
            emit_wload(0, k)

        qkb_tok = [None, None]
        trio_tok = None
        scan_ev = {}
        sch_ev = [None, None]
        ao_last = None
        scan_carry = [None]
        GORDER = [1, 2, 3, 0]
        for gi, g in enumerate(GORDER):
            heads = [g * HPG + hh for hh in range(HPG)]

            spre_last = None
            pend = []
            for pi in range(8):
                slot = 8 + pi
                w0 = [lnx_out[slot], trio_tok]
                if gi == 0 and pi == 0:
                    w0 += [tab, stg_tok[0], stg_tok[1], sq_tok[0], sq_tok[1]] + lnx_out
                t_mmv = None
                for k in range(16):
                    sl = slot_of(gi, k)
                    t_mmv = p.op("tensor", lambda e, slot=slot, k=k, sl=sl: (
                        e.matmul(P0[:], LNXT[:, slot * 2048 + k * 128:slot * 2048 + (k + 1) * 128],
                                 WRING[:, sl * 1536 + 512:sl * 1536 + 1024],
                                 start=(k == 0), stop=(k == 15), skip_group_check=True),
                        e.matmul(P1[:], LNXT[:, slot * 2048 + k * 128:slot * 2048 + (k + 1) * 128],
                                 WRING[:, sl * 1536 + 1024:sl * 1536 + 1536],
                                 start=(k == 0), stop=(k == 15), skip_group_check=True))[1],
                        waits=(w0 + [ring_dma[(gi, 15)]]) if k == 0 else [],
                        token=(k == 15))
                if pi == 0 and scan_carry[0] is not None:
                    scan_carry[0]()
                    scan_carry[0] = None
                par = pi % 2
                qoff = par * 1536
                t_ek = p.sopf("scalar", lambda e, par=par: e.activation(
                    STAGE[:, par * 1024:par * 1024 + 512], P0[:], AF.Exp),
                    waits=[t_mmv, qkb_tok[par]], token=True)
                t_km = p.op("vector", lambda e, par=par: e.tensor_scalar(
                    STAGE[:, 2048 + par * 512:2048 + (par + 1) * 512],
                    STAGE[:, par * 1024:par * 1024 + 512], 1.0, None, ALU.min),
                    waits=[t_ek], token=True)
                t_kp = p.op("vector", lambda e, qoff=qoff, par=par: e.scalar_tensor_tensor(
                    QKB[:, qoff + 512:qoff + 1024], P0[:], 0.0,
                    STAGE[:, 2048 + par * 512:2048 + (par + 1) * 512], ALU.max, ALU.add),
                    waits=[t_km], token=True)
                t_kv = p.opf("vector", lambda e, qoff=qoff: e.tensor_tensor(
                    QKB[:, qoff + 1024:qoff + 1536], QKB[:, qoff + 512:qoff + 1024],
                    P1[:], ALU.mult), waits=[t_kp], token=True)
                trio_tok = t_kv
                qkb_tok[par] = t_kv

                def mk_colsum(pi=pi, qoff=qoff, t_kv=t_kv, heads=heads, g=g):
                    nonlocal spre_last
                    for hh in range(HPG):
                        h = heads[hh]
                        first = (pi == 0 and hh == 0)
                        w1 = [t_kv]
                        if first:
                            w1 += [sch_ev[0], sch_ev[1]]
                        spre_last = p.op("tensor", lambda e, h=h, hh=hh, pi=pi, qoff=qoff: (
                            e.matmul(P4[0:1, hh * 128:(hh + 1) * 128],
                                     WCOL[:, h * 8 + pi:h * 8 + pi + 1],
                                     QKB[:, qoff + 1024 + hh * 128:qoff + 1024 + (hh + 1) * 128],
                                     start=(pi == 0 and hh == 0), stop=False,
                                     skip_group_check=True),
                            e.matmul(P6[0:1, hh * 128:(hh + 1) * 128],
                                     WCOL[:, h * 8 + pi:h * 8 + pi + 1],
                                     QKB[:, qoff + 512 + hh * 128:qoff + 512 + (hh + 1) * 128],
                                     start=(pi == 0 and hh == 0),
                                     stop=(pi == 7 and hh == HPG - 1),
                                     skip_group_check=True))[1],
                            waits=w1, token=(pi == 7 and hh == HPG - 1))
                pend.append(mk_colsum)
                if len(pend) >= 2:
                    pend.pop(0)()
                emit_abs(1)
            pend2 = []
            for t in range(8):
                par = t % 2
                qoff = par * 1536
                w0 = [lnx_out[t], trio_tok]
                t_mmv = None
                for k in range(16):
                    sl = slot_of(gi, k)
                    t_mmv = p.op("tensor", lambda e, t=t, k=k, sl=sl: (
                        e.matmul(P0[:], LNXT[:, t * 2048 + k * 128:t * 2048 + (k + 1) * 128],
                                 WRING[:, sl * 1536 + 0:sl * 1536 + 512],
                                 start=(k == 0), stop=(k == 15), skip_group_check=True),
                        e.matmul(P1[:], LNXT[:, t * 2048 + k * 128:t * 2048 + (k + 1) * 128],
                                 WRING[:, sl * 1536 + 512:sl * 1536 + 1024],
                                 start=(k == 0), stop=(k == 15), skip_group_check=True),
                        e.matmul(P2[:], LNXT[:, t * 2048 + k * 128:t * 2048 + (k + 1) * 128],
                                 WRING[:, sl * 1536 + 1024:sl * 1536 + 1536],
                                 start=(k == 0), stop=(k == 15), skip_group_check=True))[2],
                        waits=(w0 + [ring_dma[(gi, 15)]]) if k == 0 else [],
                        token=(k == 15 or t == 7))
                    if t == 7:
                        cons_tok[(gi, k)] = t_mmv
                if t == 0:
                    pend.pop(0)()
                    t_spk = p.opf("vector", lambda e, g=g: e.tensor_scalar(
                        SSB[0:1, g * 512:(g + 1) * 512], P4[0:1, :],
                        SCAL[0:1, 0:1], None, ALU.mult),
                        waits=[spre_last, t_pm], token=True)
                    t_sps = p.opf("vector", lambda e, g=g: e.tensor_scalar(
                        SSB[0:1, 2048 + g * 512:2048 + (g + 1) * 512], P6[0:1, :],
                        SCAL[0:1, 0:1], None, ALU.mult),
                        waits=[spre_last, t_pm], token=True)
                    sch_ev = [t_spk, t_sps]
                    if debug:
                        snap_p = p.dma("sync", ssbp_d[gi], SSB[0:1, :],
                                       waits=[t_spk, t_sps])
                        sch_ev = [snap_p, snap_p]
                t_eq = p.sopf("scalar", lambda e, par=par: e.activation(
                    STAGE[:, par * 1024:par * 1024 + 512], P0[:], AF.Exp),
                    waits=[t_mmv, qkb_tok[par]], token=True)
                t_ek = p.sopf("scalar", lambda e, par=par: e.activation(
                    STAGE[:, par * 1024 + 512:par * 1024 + 1024], P1[:], AF.Exp),
                    waits=[t_mmv], token=True)
                t_qm = p.op("vector", lambda e, par=par: e.tensor_scalar(
                    STAGE[:, 2048 + par * 512:2048 + (par + 1) * 512],
                    STAGE[:, par * 1024:par * 1024 + 512], 1.0, None, ALU.min),
                    waits=[t_eq], token=True)
                t_qp = p.op("vector", lambda e, qoff=qoff, par=par: e.scalar_tensor_tensor(
                    QKB[:, qoff:qoff + 512], P0[:], 0.0,
                    STAGE[:, 2048 + par * 512:2048 + (par + 1) * 512], ALU.max, ALU.add),
                    waits=[t_qm], token=True)
                t_km = p.op("vector", lambda e, par=par: e.tensor_scalar(
                    STAGE[:, 2048 + par * 512:2048 + (par + 1) * 512],
                    STAGE[:, par * 1024 + 512:par * 1024 + 1024], 1.0, None, ALU.min),
                    waits=[t_ek, t_qp], token=True)
                t_kp = p.op("vector", lambda e, qoff=qoff, par=par: e.scalar_tensor_tensor(
                    QKB[:, qoff + 512:qoff + 1024], P1[:], 0.0,
                    STAGE[:, 2048 + par * 512:2048 + (par + 1) * 512], ALU.max, ALU.add),
                    waits=[t_km], token=True)
                t_kv = p.opf("vector", lambda e, qoff=qoff: e.tensor_tensor(
                    QKB[:, qoff + 1024:qoff + 1536], QKB[:, qoff + 512:qoff + 1024],
                    P2[:], ALU.mult), waits=[t_kp], token=True)
                qkb_tok[par] = t_kv
                trio_tok = t_kv

                def mk_scan(t=t, qoff=qoff, t_kv=t_kv, heads=heads, g=g):
                    nonlocal ao_last
                    for hf in range(2):
                        sbank = P3 if hf == 0 else P5
                        xb = P4 if hf == 0 else P6
                        t_sc = None
                        t_schain = None
                        for hh2 in range(2):
                            hh = hf * 2 + hh2
                            h = heads[hh]
                            for sci in range(2):
                                poff = sci * 256 + hh2 * 128
                                rhs_off = qoff + (1024 if sci == 0 else 512) + hh * 128
                                ssb_off = sci * 2048 + g * 512 + hh * 128
                                first = (hh2 == 0 and sci == 0)
                                w1 = [t_kv, scan_ev.get(hf), sch_ev[hf]] if first else []
                                p.op("tensor", lambda e, h=h, poff=poff, rhs_off=rhs_off,
                                     first=first, sbank=sbank: e.matmul(
                                         sbank[:, poff:poff + 128],
                                         U[:, h * 128:(h + 1) * 128],
                                         QKB[:, rhs_off:rhs_off + 128],
                                         start=first, stop=False, skip_group_check=True),
                                     waits=w1, token=False)
                                t_sc = p.op("tensor", lambda e, h=h, poff=poff,
                                            ssb_off=ssb_off, sbank=sbank,
                                            last=(hh2 == 1 and sci == 1): e.matmul(
                                                sbank[:, poff:poff + 128],
                                                DPOW[0:1, h * 128:(h + 1) * 128],
                                                SSB[0:1, ssb_off:ssb_off + 128],
                                                start=False, stop=last,
                                                skip_group_check=True),
                                            waits=[], token=(hh2 == 1 and sci == 1))
                                w2 = [sch_ev[hf]] if first else []
                                t_schain = p.op("tensor", lambda e, h=h, hh2=hh2, sci=sci,
                                                rhs_off=rhs_off, ssb_off=ssb_off,
                                                first=first, xb=xb,
                                                last=(hh2 == 1 and sci == 1): (
                                    e.matmul(xb[0:1, sci * 256 + hh2 * 128:
                                             sci * 256 + (hh2 + 1) * 128],
                                             WCOL[:, h * 8 + 7:h * 8 + 8],
                                             QKB[:, rhs_off:rhs_off + 128],
                                             start=first, stop=False,
                                             skip_group_check=True),
                                    e.matmul(xb[0:1, sci * 256 + hh2 * 128:
                                             sci * 256 + (hh2 + 1) * 128],
                                             D128[0:1, h:h + 1],
                                             SSB[0:1, ssb_off:ssb_off + 128],
                                             start=False, stop=last,
                                             skip_group_check=True))[1],
                                    waits=w2, token=(hh2 == 1 and sci == 1))
                        hl = (heads[hf * 2] - g * HPG)
                        t_s1 = p.op("vector", lambda e, g=g, hl=hl, xb=xb: e.tensor_copy(
                            SSB[0:1, g * 512 + hl * 128:g * 512 + hl * 128 + 256],
                            xb[0:1, 0:256]), waits=[t_schain, t_sc], token=True)
                        t_s2 = p.opf("vector", lambda e, g=g, hl=hl, xb=xb: e.tensor_copy(
                            SSB[0:1, 2048 + g * 512 + hl * 128:
                                2048 + g * 512 + hl * 128 + 256],
                            xb[0:1, 256:512]), waits=[t_schain], token=True)
                        sch_ev[hf] = t_s2
                        t_qk = p.op("vector", lambda e, qoff=qoff, hf=hf, sbank=sbank:
                                    e.tensor_tensor(
                                        STAGE[:, 3072:3328],
                                        QKB[:, qoff + hf * 256:qoff + hf * 256 + 256],
                                        sbank[:, 256:512], ALU.mult),
                                    waits=[t_sc], token=True)
                        t_o = p.op("vector", lambda e, qoff=qoff, hf=hf, sbank=sbank:
                                   e.tensor_tensor(
                                       STAGE[:, 3328:3584],
                                       QKB[:, qoff + hf * 256:qoff + hf * 256 + 256],
                                       sbank[:, 0:256], ALU.mult),
                                   waits=[t_qk], token=True)
                        t_dn = p.op("vector", lambda e: e.tensor_reduce(
                            STATS[:, 32:34], STAGE[:, 3072:3328].rearrange(
                                "p (g d) -> p g d", g=2), AX.X, ALU.add),
                            waits=[t_qk], token=True)
                        p.vflush()
                        t_cl = p.op("vector", lambda e: e.tensor_scalar(
                            STATS[:, 34:36], STATS[:, 32:34], EPS_DEN, None, ALU.max),
                            waits=[t_dn], token=True)
                        p.vflush()
                        t_rc = p.op("vector", lambda e: e.reciprocal(
                            STATS[:, 36:38], STATS[:, 34:36]), waits=[t_cl], token=True)
                        p.vflush()
                        last = None
                        for hh2 in range(2):
                            h = heads[hf * 2 + hh2]
                            emitfn = p.opf if hh2 == 1 else p.op
                            last = emitfn("vector", lambda e, t=t, h=h, hh2=hh2:
                                          e.tensor_scalar(
                                              AO[:, t * 2048 + h * 128:t * 2048 + (h + 1) * 128],
                                              STAGE[:, 3328 + hh2 * 128:3328 + (hh2 + 1) * 128],
                                              STATS[:, 36 + hh2:37 + hh2], None, ALU.mult),
                                          waits=[t_rc, t_o], token=(hh2 == 1))
                        scan_ev[hf] = last
                        ao_last = last
                pend2.append(mk_scan)
                if len(pend2) >= 2:
                    pend2.pop(0)()
                emit_abs(2)
            scan_carry[0] = pend2.pop(0)
            assert not pend2
            if debug:
                snap_tok = p.dma("sync", ao_g[gi], AO[:], waits=[ao_last])
                trio_tok_gate = snap_tok
            else:
                trio_tok_gate = None
            if trio_tok_gate is not None:
                trio_tok = trio_tok_gate
            if gi + 1 < NG:
                for k in range(16):
                    emit_wload(gi + 1, k, extra=ao_last)

        t_fill = None
        for df in range(48):
            t_fill = p.op("tensor", lambda e: e.matmul(
                P2[:], EYE[:], WRING[:, 0:512], start=True, stop=True,
                skip_group_check=True),
                waits=[trio_tok] if df == 0 else [], token=(df == 47))
        scan_carry[0]()
        scan_carry[0] = None

        dbg_gate = None
        if debug and ao_d is not None:
            for t in range(NT):
                p.dma("sync", ao_d[t], AO[:, t * 2048:(t + 1) * 2048], waits=[ao_last])
            p.dma("sync", lnxt_d, LNXT[:], waits=[ao_last])
            p.dma("sync", u_d, U[:], waits=[ao_last])
            p.dma("sync", dpow_d, DPOW[:], waits=[ao_last])
            p.dma("sync", wcol_d, WCOL[:], waits=[ao_last])
            p.dma("sync", lnd_d, LND[:], waits=[ao_last])
            dbg_gate = p.dma("sync", ssb_d, SSB[:], waits=[ao_last])

        wo_last = None
        for k in range(16):
            dst = (WOA[:, k * 2048:(k + 1) * 2048] if k < 12
                   else WOB[:, (k - 12) * 2048:(k - 11) * 2048])
            wo_last = p.dma("gpsimd", dst, wouta[k],
                            waits=[cons_tok[(NG - 1, k)], ao_last, dbg_gate])
        aoT_tok = [None, None]
        x2_tok = [None] * NT
        proj_v = None
        x2dma_prev = None
        for t in range(NT):
            buf = t % 2
            evs = transpose16(
                (lambda jj, t=t: AO[:, t * 2048 + jj * 128:t * 2048 + (jj + 1) * 128]),
                (lambda half, buf=buf: (
                    JK[:, buf * 2048 + half * 1024:buf * 2048 + (half + 1) * 1024],
                    PT[:])),
                ao_last, extra_wait=aoT_tok[buf])
            t_x = p.dma("sync", STAGE[:, 0:DM], xw[t], waits=[proj_v, ao_last])
            mm_last = [None] * 4
            for k in range(16):
                rhs = (WOA[:, k * 2048:(k + 1) * 2048] if k < 12
                       else WOB[:, (k - 12) * 2048:(k - 11) * 2048])
                for n in range(4):
                    w0 = (evs + [wo_last, proj_v]) if (k == 0 and n == 0) else []
                    mm_last[n] = p.op("tensor", lambda e, buf=buf, k=k, n=n, rhs=rhs:
                                      e.matmul(
                                          PB[n][:],
                                          JK[:, buf * 2048 + k * 128:
                                             buf * 2048 + (k + 1) * 128],
                                          rhs[:, n * 512:(n + 1) * 512],
                                          start=(k == 0), stop=(k == 15),
                                          skip_group_check=True),
                                      waits=w0, token=(k == 15))
            aoT_tok[buf] = mm_last[3]
            t_r = None
            for n in range(4):
                t_r = p.op("vector", lambda e, n=n: e.tensor_tensor(
                    STAGE[:, 2048 + n * 512:2048 + (n + 1) * 512],
                    STAGE[:, n * 512:(n + 1) * 512], PB[n][:], ALU.add),
                    waits=[mm_last[n], t_x, x2dma_prev] if n == 0 else [mm_last[n]],
                    token=(n == 3))
            proj_v = t_r
            x2_tok[t] = p.dma("sync", x2d[t], STAGE[:, 2048:4096], waits=[t_r])
            x2dma_prev = x2_tok[t]
            emit_abs(4)

        emit_abs(300)
        t_swm = p.op("tensor", lambda e: e.matmul(
            P4[0:1, 0:192], ONES[:], ABSACC[:, 0:192], start=True, stop=True),
            waits=[abs_st["tok"][0], abs_st["tok"][1], t_ones,
                   sch_ev[0], sch_ev[1], ao_last], token=True)
        t_r1 = p.op("vector", lambda e: e.reduce_sum(
            SCAL[0:1, 2:3], P4[0:1, 0:128], AX.X), waits=[t_swm], token=True)
        t_r2 = p.op("vector", lambda e: e.reduce_sum(
            SCAL[0:1, 3:4], P4[0:1, 128:192], AX.X), waits=[t_swm], token=True)
        t_s1 = p.op("vector", lambda e: e.tensor_scalar(
            SCAL[0:1, 4:5], SCAL[0:1, 2:3], 1.0 / WGV_ELEMS, 1e-5, ALU.mult, ALU.max),
            waits=[t_r1], token=True)
        t_s2 = p.op("vector", lambda e: e.tensor_scalar(
            SCAL[0:1, 5:6], SCAL[0:1, 3:4], 1.0 / WOF_ELEMS, 1e-5, ALU.mult, ALU.max),
            waits=[t_r2], token=True)
        t_i1 = p.op("vector", lambda e: e.reciprocal(
            SCAL[0:1, 6:7], SCAL[0:1, 4:5]), waits=[t_s1], token=True)
        t_i2 = p.opf("vector", lambda e: e.reciprocal(
            SCAL[0:1, 7:8], SCAL[0:1, 5:6]), waits=[t_s2], token=True)
        t_or2 = p.opf("vector", lambda e: e.memset(ONESRP[:], 1.0),
                      waits=[ao_last, sch_ev[0], sch_ev[1], t_swm], token=True)
        t_bcg = p.tok_after([t_i1, t_i2, t_or2, scan_ev.get(1)])
        t_bcm = p.op("tensor", lambda e: e.matmul(
            P5[:, 0:4], ONESRP[:], SCAL[0:1, 4:8], start=True, stop=True),
            waits=[t_bcg], token=True)
        t_bc = p.opf("vector", lambda e: e.tensor_copy(SCAL[:, 8:12], P5[:, 0:4]),
                     waits=[t_bcm], token=True)
        SW1, SW2, ISW1, ISW2 = 8, 9, 10, 11

        y_final = None
        sup_last = None
        ring_ffn = [t_swm, t_swm]
        jk_wint = None
        jk_tb = None
        wt_cons = None
        wt_store = None
        wotq_store = [None] * 4
        wotq_cons = None

        for sup in range(2):
            tiles = [sup * 4 + i for i in range(4)]
            if sup == 0:
                t_g0 = p.dma("sync", GREP[0:1, :], gwin, waits=[proj_v])
            else:
                t_g0 = p.dma("sync", GREP[0:1, :], gwin,
                             waits=[sup_last, wt_store, wt_cons])
            t_gm = [None] * 4
            t_gc = [None] * 4
            t_gg = p.tok_after([t_g0, t_bc])
            for nn in range(4):
                w0 = [t_gg] if nn < 2 else [t_gc[nn - 2]]
                t_gm[nn] = p.op("tensor", lambda e, nn=nn: e.matmul(
                    (P4 if nn % 2 == 0 else P5)[:], ONESRP[:],
                    GREP[0:1, nn * 512:(nn + 1) * 512], start=True, stop=True),
                    waits=w0, token=True)
                t_gc[nn] = p.opf("vector", lambda e, nn=nn: e.tensor_copy(
                    GREP[:, nn * 512:(nn + 1) * 512],
                    (P4 if nn % 2 == 0 else P5)[:]), waits=[t_gm[nn]], token=True)
            t_grep = t_gc[3]
            a1T_ev = [None] * 4
            grep_free = None
            stg3 = [None, None]
            sq3 = [None, None]
            if sup == 0:
                x2u_tok = [None, None]
            wk_free = None
            for ti, t in enumerate(tiles):
                buf = ti % 2
                xoff = buf * 2048
                w0 = [x2_tok[t], t_grep, stg3[buf], sq3[1 - buf], x2u_tok[buf]]
                if sup == 1:
                    w0 += [sup_last, wotq_store[3], wt_store] + sup_out_toks
                t_x = p.dma("sync", STAGE[:, xoff:xoff + DM], x2d[t], waits=w0)
                t_sum = p.op("vector", lambda e, xoff=xoff: e.reduce_sum(
                    STATS[:, 0:1], STAGE[:, xoff:xoff + DM], AX.X),
                    waits=[t_x], token=True)
                t_sq = p.op("scalar", lambda e, xoff=xoff, buf=buf: e.activation(
                    STAGE[:, (1 - buf) * 2048:(1 - buf) * 2048 + DM],
                    STAGE[:, xoff:xoff + DM], AF.Square, accum_out=STATS[:, 1:2]),
                    waits=[t_x, stg3[1 - buf]], token=True)
                sq3[buf] = t_sq
                p.vdelay([t_sq])
                t_b = p.op("vector", lambda e: e.tensor_scalar(
                    STATS[:, 5:6], STATS[:, 1:2], 1.0 / DM, EPS_LN, ALU.mult, ALU.add),
                    waits=[t_sq], token=True)
                t_a = p.op("vector", lambda e: e.tensor_scalar(
                    STATS[:, 4:5], STATS[:, 0:1], -1.0 / DM, None, ALU.mult),
                    waits=[t_sum], token=True)
                p.vflush()
                t_c = p.op("vector", lambda e: e.tensor_tensor(
                    STATS[:, 6:7], STATS[:, 4:5], STATS[:, 4:5], ALU.mult),
                    waits=[t_a], token=True)
                p.vflush()
                t_d2 = p.opf("vector", lambda e: e.tensor_tensor(
                    STATS[:, 7:8], STATS[:, 5:6], STATS[:, 6:7], ALU.subtract),
                    waits=[t_b, t_c], token=True)
                p.sdelay([t_d2])
                t_e = p.op("scalar", lambda e: e.activation(
                    STATS[:, 8:9], STATS[:, 7:8], AF.Ln), waits=[t_d2], token=True)
                t_f = p.op("scalar", lambda e: e.activation(
                    STATS[:, 9:10], STATS[:, 8:9], AF.Exp, scale=-0.5),
                    waits=[t_e], token=True)
                p.vdelay([t_f])
                t_n = p.op("vector", lambda e, xoff=xoff: e.tensor_scalar(
                    WK[:], STAGE[:, xoff:xoff + DM], STATS[:, 4:5], STATS[:, 9:10],
                    ALU.add, ALU.mult), waits=[t_f, t_sq, wk_free, ring_ffn[0],
                                               ring_ffn[1]], token=True)
                t_sq2 = p.op("scalar", lambda e: e.activation(
                    SPX[:], WK[:], AF.Square,
                    accum_out=STATS[:, 10:11]), waits=[t_n], token=True)
                t_xu = p.opf("vector", lambda e, xoff=xoff, buf=buf: e.tensor_tensor(
                    STAGE[:, (1 - buf) * 2048:(1 - buf) * 2048 + DM],
                    STAGE[:, xoff:xoff + DM], WK[:], ALU.add),
                    waits=[sq3[1 - buf], x2u_tok[1 - buf]], token=True)
                x2u_tok[1 - buf] = p.dma(
                    "scalar", x2ud[t],
                    STAGE[:, (1 - buf) * 2048:(1 - buf) * 2048 + DM], waits=[t_xu])
                sq3[buf] = t_sq2
                p.vdelay([t_sq2])
                t_g = p.opf("vector", lambda e: e.tensor_scalar(
                    STATS[:, 11:12], STATS[:, 10:11], 1.0 / DM, EPS_LN, ALU.mult,
                    ALU.add), waits=[t_sq2], token=True)
                p.sdelay([t_g])
                t_h1 = p.op("scalar", lambda e: e.activation(
                    STATS[:, 12:13], STATS[:, 11:12], AF.Ln), waits=[t_g], token=True)
                t_h2 = p.op("scalar", lambda e: e.activation(
                    STATS[:, 13:14], STATS[:, 12:13], AF.Exp, scale=-0.5),
                    waits=[t_h1], token=True)
                p.vdelay([t_h2])
                t_z1 = p.op("vector", lambda e: e.tensor_scalar(
                    WK[:], WK[:], STATS[:, 13:14], None, ALU.mult),
                    waits=[t_h2], token=True)
                t_z2 = p.op("vector", lambda e: e.tensor_tensor(
                    WK[:], WK[:], GREP[:], ALU.mult), waits=[t_z1, t_grep], token=True)
                grep_free = t_z2
                t_m1 = p.op("vector", lambda e: e.reduce_max(
                    STATS[:, 14:15], WK[:], AX.X, apply_absolute_value=True),
                    waits=[t_z2], token=True)
                p.vflush()
                t_cl = p.op("vector", lambda e, ti=ti: e.tensor_scalar(
                    STATS[:, 16 + ti:17 + ti], STATS[:, 14:15], 1e-5, None, ALU.max),
                    waits=[t_m1], token=True)
                p.vflush()
                t_rc = p.op("vector", lambda e, ti=ti: e.reciprocal(
                    STATS[:, 15:16], STATS[:, 16 + ti:17 + ti]), waits=[t_cl],
                    token=True)
                p.vflush()
                t_sc1 = p.op("vector", lambda e, ti=ti: e.tensor_scalar(
                    STATS[:, 20 + ti:21 + ti], STATS[:, 15:16], QB, None, ALU.mult),
                    waits=[t_rc], token=True)
                p.opf("vector", lambda e, ti=ti: e.scalar_tensor_tensor(
                    STATS[:, 24 + ti:25 + ti], STATS[:, 16 + ti:17 + ti], 1.0 / QB,
                    SCAL[:, SW1:SW1 + 1], ALU.mult, ALU.mult),
                    waits=[t_sc1, t_bc], token=True)
                t_q1 = p.op("vector", lambda e, xoff=xoff, ti=ti: e.tensor_scalar(
                    STAGE[:, xoff:xoff + DM], WK[:], STATS[:, 20 + ti:21 + ti], RC,
                    ALU.mult, ALU.add), waits=[t_sc1, t_sq2], token=True)
                wk_free = t_q1
                t_q2 = p.opf("vector", lambda e, xoff=xoff: e.tensor_scalar(
                    JK[:, 0:2048], STAGE[:, xoff:xoff + DM], RC, None, ALU.subtract),
                    waits=[t_q1, jk_wint], token=True)
                stg3[buf] = t_q2
                evs = transpose16(
                    (lambda jj: JK[:, jj * 128:(jj + 1) * 128]),
                    (lambda half, ti=ti: (
                        A1T[:, ti * 2048 + half * 1024:ti * 2048 + (half + 1) * 1024],
                        PT[:])),
                    t_q2)
                jk_wint = evs[-1]
                a1T_ev[ti] = evs[-1]

            h_rmax = [None] * 4
            ffn1_ev = {}
            for hg in range(32):
                if sup == 0:
                    wt_ready = None
                    for c in range(4):
                        chunk = (2 * hg + c) if c < 2 else (62 + 2 * hg + c)
                        sl = c % 2
                        roff = sl * 2048
                        t_wd = p.dma("sync", STREAM[:, roff:roff + DM], wgv[chunk],
                                     waits=[ring_ffn[sl], stg3[0], stg3[1],
                                            x2u_tok[0], x2u_tok[1]])
                        if hg == 0 and c == 0:
                            p.sdelay([t_bc])
                        t_qa = p.op("scalar", lambda e, roff=roff: e.activation(
                            WK[:], STREAM[:, roff:roff + DM], AF.Identity, bias=RC,
                            scale=SCAL[:, ISW1:ISW1 + 1]),
                            waits=[t_wd, t_bc, wk_free], token=True)
                        ring_ffn[sl] = t_qa
                        t_qb = p.op("vector", lambda e: e.tensor_scalar(
                            JK[:, 2048:4096], WK[:], RC, 1.0, ALU.subtract, ALU.min),
                            waits=[t_qa, jk_tb], token=True)
                        wk_free = t_qb
                        t_qc = p.opf("vector", lambda e: e.tensor_scalar(
                            JK[:, 0:2048], JK[:, 2048:4096], -1.0, None, ALU.max),
                            waits=[t_qb, jk_wint], token=True)
                        jk_tb = t_qc
                        evs = transpose16(
                            (lambda jj: JK[:, jj * 128:(jj + 1) * 128]),
                            (lambda half, c=c: (
                                WT[:].rearrange("p (k n) -> p k n", k=16)[
                                    :, half * 8:(half + 1) * 8, c * 128:(c + 1) * 128],
                                PT[:].rearrange("p (k n) -> p k n", k=8))),
                            t_qc, extra_wait=(wt_cons if c == 0 else None),
                            evac_wait=(wt_store if c == 0 else None))
                        jk_wint = evs[-1]
                        wt_ready = evs[-1]
                    t_st = p.dma("gpsimd", wgvT_d[hg], WT[:], waits=[wt_ready])
                    wt_store = t_st
                    if debug and sup == 0 and hg == 0:
                        wt_store = p.dma("sync", wt_d, WT[:], waits=[wt_ready])
                else:
                    wt_ready = p.dma("sync", WT[:], wgvT_d[hg],
                                     waits=[wt_cons, sup_last, wt_store] +
                                           ([grep_free] if hg == 0 else []))
                mm_t = [None] * 4
                for ti in range(4):
                    for k in range(16):
                        w0 = []
                        if k == 0:
                            w0 = [wt_ready, a1T_ev[ti], ffn1_ev.get((hg - 1, ti))]
                        mm_t[ti] = p.op("tensor", lambda e, ti=ti, k=k: e.matmul(
                            PB[ti][:],
                            A1T[:, ti * 2048 + k * 128:ti * 2048 + (k + 1) * 128],
                            WT[:, k * 512:(k + 1) * 512],
                            start=(k == 0), stop=(k == 15), skip_group_check=True),
                            waits=w0, token=(k == 15))
                wt_cons = mm_t[3]
                for ti in range(4):
                    t_sg2 = p.op("scalar", lambda e, ti=ti: e.activation(
                        SILU[:], PB[ti][:, 256:512], AF.Silu,
                        scale=STATS[:, 24 + ti:25 + ti]),
                        waits=[mm_t[ti]] + ([proj_v] if (sup == 0 and hg == 0
                                                         and ti == 0) else []),
                        token=True)
                    t_h = p.op("vector", lambda e, ti=ti, hg=hg: e.scalar_tensor_tensor(
                        H[:, ti * 8192 + hg * 256:ti * 8192 + (hg + 1) * 256],
                        PB[ti][:, 0:256], STATS[:, 24 + ti:25 + ti], SILU[:],
                        ALU.mult, ALU.mult), waits=[t_sg2], token=True)
                    ffn1_ev[(hg, ti)] = t_h
                    t_rm = p.op("vector", lambda e, ti=ti, hg=hg: e.reduce_max(
                        STATS[:, 40 + ti:41 + ti],
                        H[:, ti * 8192 + hg * 256:ti * 8192 + (hg + 1) * 256],
                        AX.X, apply_absolute_value=True), waits=[t_h], token=True)
                    p.vflush()
                    if hg == 0:
                        h_rmax[ti] = p.op("vector", lambda e, ti=ti: e.tensor_copy(
                            STATS[:, 28 + ti:29 + ti], STATS[:, 40 + ti:41 + ti]),
                            waits=[t_rm], token=True)
                    else:
                        h_rmax[ti] = p.op("vector", lambda e, ti=ti: e.tensor_tensor(
                            STATS[:, 28 + ti:29 + ti], STATS[:, 28 + ti:29 + ti],
                            STATS[:, 40 + ti:41 + ti], ALU.max), waits=[t_rm], token=True)

            h3tok = None
            if debug:
                h3tok = p.dma("sync", h3_d if sup == 0 else h3b_d,
                              H[:, 3 * 8192:4 * 8192],
                              waits=[ffn1_ev[(31, 3)], h_rmax[3]])
            a2T_ev = [None] * 4
            for ti in range(4):
                t_cl2 = p.op("vector", lambda e, ti=ti: e.tensor_scalar(
                    STATS[:, 32 + ti:33 + ti], STATS[:, 28 + ti:29 + ti], 1e-5, None,
                    ALU.max), waits=[h_rmax[ti]], token=True)
                p.vflush()
                t_rc2 = p.op("vector", lambda e, ti=ti: e.reciprocal(
                    STATS[:, 44:45], STATS[:, 32 + ti:33 + ti]), waits=[t_cl2],
                    token=True)
                p.vflush()
                t_sc2 = p.op("vector", lambda e, ti=ti: e.tensor_scalar(
                    STATS[:, 36 + ti:37 + ti], STATS[:, 44:45], QB, None, ALU.mult),
                    waits=[t_rc2], token=True)
                p.vflush()
                p.op("vector", lambda e, ti=ti: e.scalar_tensor_tensor(
                    STATS[:, 48 + ti:49 + ti], STATS[:, 32 + ti:33 + ti], 1.0 / QB,
                    SCAL[:, SW2:SW2 + 1], ALU.mult, ALU.mult),
                    waits=[t_sc2, t_bc], token=True)
                t_q2b = None
                for j in range(4):
                    t_qa = p.op("vector", lambda e, ti=ti, j=j: e.tensor_scalar(
                        WK[:], H[:, ti * 8192 + j * 2048:ti * 8192 + (j + 1) * 2048],
                        STATS[:, 36 + ti:37 + ti], RC, ALU.mult, ALU.add),
                        waits=[t_sc2, wotq_cons, wk_free, h3tok,
                               wotq_store[0], wotq_store[1],
                               wotq_store[2], wotq_store[3]], token=True)
                    t_q2b = p.opf("vector", lambda e, ti=ti, j=j: e.tensor_scalar(
                        A2[ti][:, j * 2048:(j + 1) * 2048], WK[:], RC, None,
                        ALU.subtract), waits=[t_qa], token=True)
                wk_free = t_q2b
                evs = transpose16(
                    (lambda jj, ti=ti: A2[ti][:, jj * 128:(jj + 1) * 128]),
                    (lambda half, ti=ti: (
                        A2T[ti][:, half * 1024:(half + 1) * 1024], PT[:])),
                    t_q2b, extra_wait=wotq_cons, nblk=64)
                a2T_ev[ti] = evs[-1]

            if debug and sup == 0:
                p.dma("sync", scal_d, SCAL[:], waits=[a2T_ev[0], a2T_ev[3]])
                p.dma("sync", stats_d, STATS[:], waits=[a2T_ev[0], a2T_ev[3]])
                p.dma("sync", h_d, H[:, 0:8192], waits=[a2T_ev[0], a2T_ev[3]])
                p.dma("sync", a1t_d, A1T[:, 0:8192], waits=[a2T_ev[0], a2T_ev[3]])
                dtok = p.dma("sync", a2t_d, A2T[3][:], waits=[a2T_ev[0], a2T_ev[3]])
                q_mm_prev = dtok
            if debug and sup == 1:
                p.dma("sync", statsb_d, STATS[:], waits=[a2T_ev[0], a2T_ev[3]])
                dtok = p.dma("sync", a2tb_d, A2T[3][:], waits=[a2T_ev[0], a2T_ev[3]])
                q_mm_prev = dtok

            q_mm_prev = None
            out_ev_prev = [None] * 4
            out_slot_tok = [None, None]
            for q in range(4):
                wotq_ready = []
                if sup == 0:
                    for c in range(4):
                        chunk = 4 * q + c
                        for j in range(4):
                            sl = j % 2
                            roff = sl * 2048
                            t_wd = p.dma("sync", STREAM[:, roff:roff + DM],
                                         woutf[chunk][:, j * 2048:(j + 1) * 2048],
                                         waits=[ring_ffn[sl]])
                            t_qa = p.op("scalar", lambda e, roff=roff: e.activation(
                                WK[:], STREAM[:, roff:roff + DM], AF.Identity, bias=RC,
                                scale=SCAL[:, ISW2:ISW2 + 1]),
                                waits=[t_wd, wk_free], token=True)
                            ring_ffn[sl] = t_qa
                            t_qb = p.op("vector", lambda e: e.tensor_scalar(
                                JK[:, 2048:4096], WK[:], RC, 1.0, ALU.subtract,
                                ALU.min), waits=[t_qa, jk_tb], token=True)
                            wk_free = t_qb
                            t_qc = p.opf("vector", lambda e: e.tensor_scalar(
                                JK[:, 0:2048], JK[:, 2048:4096], -1.0, None, ALU.max),
                                waits=[t_qb, jk_wint], token=True)
                            jk_tb = t_qc
                            evs = transpose16(
                                (lambda jj: JK[:, jj * 128:(jj + 1) * 128]),
                                (lambda half, j=j, c=c: (
                                    WOTQ[j][:].rearrange("p (k n) -> p k n", k=16)[
                                        :, half * 8:(half + 1) * 8,
                                        c * 128:(c + 1) * 128],
                                    PT[:].rearrange("p (k n) -> p k n", k=8))),
                                t_qc,
                                extra_wait=([q_mm_prev, wotq_store[j]]
                                            if (c == 0 and j == 0)
                                            else [wotq_store[j]]))
                            jk_wint = evs[-1]
                            if c == 3:
                                wotq_ready.append(evs[-1])
                    for j in range(4):
                        wotq_store[j] = p.dma("gpsimd", woT_d[4 * q + j], WOTQ[j][:],
                                              waits=[wotq_ready[j]])
                    if debug and sup == 0 and q == 0:
                        wotq_ready.append(p.dma("sync", wotq_d, WOTQ[0][:],
                                                waits=[wotq_ready[0]]))
                else:
                    for j in range(4):
                        wotq_ready.append(
                            p.dma("sync", WOTQ[j][:], woT_d[4 * q + j],
                                  waits=[q_mm_prev, sup_last, a2T_ev[3]]))
                mm_t = [None] * 4
                for ti in range(4):
                    for kk in range(64):
                        j, ki = kk // 16, kk % 16
                        w0 = []
                        if kk == 0:
                            w0 = [a2T_ev[ti], out_ev_prev[ti]] + wotq_ready
                        mm_t[ti] = p.op("tensor", lambda e, ti=ti, j=j, ki=ki, kk=kk:
                                        e.matmul(
                            PB[ti][:],
                            A2T[ti][:, (j * 16 + ki) * 128:(j * 16 + ki + 1) * 128],
                            WOTQ[j][:, ki * 512:(ki + 1) * 512],
                            start=(kk == 0), stop=(kk == 63), skip_group_check=True),
                            waits=w0, token=(kk == 63))
                q_mm_prev = mm_t[3]
                for ti in range(4):
                    t = tiles[ti]
                    osl = ti % 2
                    t_xs = p.dma("sync", SPX[:, ti * 512:(ti + 1) * 512],
                                 x2ud[t][:, q * 512:(q + 1) * 512],
                                 waits=[out_ev_prev[ti], sq3[0], sq3[1]])
                    t_o = p.op("vector", lambda e, ti=ti, osl=osl: e.scalar_tensor_tensor(
                        SPO[:, osl * 512:(osl + 1) * 512], PB[ti][:],
                        STATS[:, 48 + ti:49 + ti], SPX[:, ti * 512:(ti + 1) * 512],
                        ALU.mult, ALU.add),
                        waits=[mm_t[ti], t_xs, out_slot_tok[osl]], token=True)
                    out_slot_tok[osl] = None
                    tok = p.dma(
                        "sync", y[t][:, q * 512:(q + 1) * 512],
                        SPO[:, osl * 512:(osl + 1) * 512], waits=[t_o])
                    out_ev_prev[ti] = tok
                    out_slot_tok[osl] = tok
                    y_final = tok
            wotq_cons = q_mm_prev
            sup_last = q_mm_prev
            sup_out_toks = [out_slot_tok[0], out_slot_tok[1]]

        p.op("sync", lambda e: e.wait_ge(sems["sync_dma"], p.cnt["sync_dma"]),
             waits=[y_final])
        p.op("gpsimd", lambda e: e.wait_ge(sems["gpsimd_dma"], p.cnt["gpsimd_dma"]))
        p.op("scalar", lambda e: e.wait_ge(sems["scalar_dma"], p.cnt.get("scalar_dma", 0)))

        p.build(block, sems)
    import concourse.mybir as _mb
    mine = set()
    for alloc in nc.m.functions[0].allocations:
        if isinstance(alloc, _mb.MemoryLocationSet):
            for ml in alloc.memorylocations:
                a = ml.addr if isinstance(ml.addr, int) else None
                nm = str(ml.name)
                if (a is not None and str(ml.type).lower().find("sb") >= 0
                        and nm.startswith("const-")):
                    assert a < 16512, f"auto const {nm} at {a} inside manual map"
    return nc


_NC_CACHE = {}


def kernel(L0_attn_x, L0_attn_w_qkv, L0_attn_w_out, L0_attn_decay_param,
           L0_ffn_norm_w, L0_ffn_w_gv, L0_ffn_w_out):
    x = np.ascontiguousarray(np.asarray(L0_attn_x, dtype=np.float32))
    w_qkv = np.ascontiguousarray(np.asarray(L0_attn_w_qkv, dtype=np.float32)
                                 ).reshape(16, 128, 3 * DM)
    w_out = np.ascontiguousarray(np.asarray(L0_attn_w_out, dtype=np.float32)
                                 ).reshape(16, 128, DM)
    w_gv = np.ascontiguousarray(np.asarray(L0_ffn_w_gv, dtype=np.float32)
                                ).reshape(128, 128, DM)
    w_of = np.ascontiguousarray(np.asarray(L0_ffn_w_out, dtype=np.float32)
                                ).reshape(16, 128, HID)
    dp = np.ascontiguousarray(np.asarray(L0_attn_decay_param,
                                         dtype=np.float32).reshape(1, NH))
    gw = np.ascontiguousarray(np.asarray(L0_ffn_norm_w,
                                         dtype=np.float32).reshape(1, DM))

    if "nc" not in _NC_CACHE:
        _NC_CACHE["nc"] = build_nc(debug=False)
    nc = _NC_CACHE["nc"]

    in_maps = []
    for c in range(8):
        b, hf = c // 2, c % 2
        xwin = np.ascontiguousarray(x[b, hf * 1024:(hf + 1) * 1024].reshape(NT, 128, DM))
        xpre = np.ascontiguousarray(x[b, 0:1024].reshape(NT, 128, DM))
        pm = np.full((1, 1), 1.0 if hf == 1 else 0.0, dtype=np.float32)
        in_maps.append({"xw": xwin, "xp": xpre, "wqkv": w_qkv, "wouta": w_out,
                        "wgv": w_gv, "woutf": w_of, "dp": dp, "gw": gw, "pmask": pm})

    res = run_bass_kernel_spmd(nc, in_maps, core_ids=list(range(8)))
    out = np.empty((4, 2048, DM), dtype=np.float32)
    for c in range(8):
        b, hf = c // 2, c % 2
        out[b, hf * 1024:(hf + 1) * 1024] = res.results[c]["y"].reshape(1024, DM)
    return out
